# version 02 (final)
# baseline (speedup 1.0000x reference)
import os
import numpy as np
import ml_dtypes
from contextlib import ExitStack

import concourse.bass as bass
import concourse.mybir as mybir
from concourse.bass_utils import run_bass_kernel_spmd

F32 = mybir.dt.float32
BF16 = mybir.dt.bfloat16
AX = mybir.AxisListType.X
OP = mybir.AluOpType
ACT = mybir.ActivationFunctionType

B, N, C = 4, 4096, 1024
H, D = 16, 64
OWN = 2048
NCH = 8
PAIRS = 8
KCN = 32
QG = 512
NQG = OWN // QG
E, I, SI = 8, 512, 1024

ENGINES = ("sync", "gpsimd", "tensor", "scalar", "vector")
SEMS = ("dg", "ds", "t", "s", "v", "c", "o")


class Plan:
    def __init__(self, serialize=False):
        self.steps = {e: [] for e in ENGINES}
        self.cnt = {s: 0 for s in SEMS}
        self.serialize = serialize

    def emit(self, eng, fn, waits=None, incs=()):
        w = {k: v for k, v in (waits or {}).items() if v}
        if self.serialize:
            for k, v in self.cnt.items():
                if v > 0:
                    w[k] = max(w.get(k, 0), v)
        self.steps[eng].append((fn, w, list(incs)))
        for s, n in incs:
            self.cnt[s] += n

    def dma(self, eng, out, in_, sem, waits=None):
        self.emit(eng, (lambda e, o=out, i=in_: e.dma_start(out=o, in_=i)),
                  waits=waits, incs=[(sem, 16)])
        return self.cnt[sem]

    def mm(self, out, lhsT, rhs, start, stop, waits=None, inc=False):
        self.emit("tensor",
                  (lambda e, o=out, l=lhsT, r=rhs, st=start, sp=stop:
                   e.matmul(o, l, r, start=st, stop=sp)),
                  waits=waits, incs=[("t", 1)] if inc else [])
        return self.cnt["t"]

    def tnop(self):
        self.emit("tensor", (lambda e: e.nop()), incs=[("t", 1)])
        return self.cnt["t"]

    def tr(self, out, in_, ident, waits=None):
        self.emit("tensor",
                  (lambda e, o=out, i=in_, d=ident: e.transpose(o, i, d)),
                  waits=waits, incs=[("t", 1)])
        return self.cnt["t"]

    def act(self, out, in_, func, bias=0.0, scale=1.0, waits=None):
        self.emit("scalar",
                  (lambda e, o=out, i=in_, f=func, b=bias, sc=scale:
                   e.activation(o, i, f, bias=b, scale=sc)),
                  waits=waits, incs=[("s", 1)])
        return self.cnt["s"]

    def scopy(self, out, in_, waits=None):
        self.emit("scalar", (lambda e, o=out, i=in_: e.copy(o, i)),
                  waits=waits, incs=[("s", 1)])
        return self.cnt["s"]

    def smul(self, out, in_, const, waits=None):
        self.emit("scalar", (lambda e, o=out, i=in_, cc=const: e.mul(o, i, cc)),
                  waits=waits, incs=[("s", 1)])
        return self.cnt["s"]

    def v(self, fn, waits=None):
        self.emit("vector", fn, waits=waits, incs=[("v", 1)])
        return self.cnt["v"]

    def tt(self, out, a, b, op, waits=None):
        return self.v(lambda e, o=out, x=a, y=b, p=op: e.tensor_tensor(o, x, y, p),
                      waits=waits)

    def ts(self, out, a, scalar, op, waits=None):
        return self.v(lambda e, o=out, x=a, s=scalar, p=op:
                      e.tensor_single_scalar(o, x, s, p), waits=waits)

    def vcopy(self, out, in_, waits=None):
        return self.v(lambda e, o=out, i=in_: e.tensor_copy(o, i), waits=waits)

    def vrec(self, out, in_, waits=None):
        def fn(e, o=out, i=in_):
            if o.dtype != F32:
                with e.bass.allow_low_precision(reason="bf16 recip for bcast matmul"):
                    return e.reciprocal(o, i)
            return e.reciprocal(o, i)
        return self.v(fn, waits=waits)

    def vrmax(self, out, in_, waits=None):
        return self.v(lambda e, o=out, i=in_: e.reduce_max(o, i, axis=AX),
                      waits=waits)


def build_program(upto="full", serialize=False, debug=False):
    nc = bass.Bass("TRN2", target_bir_lowering=False, detect_race_conditions=False)
    pl = Plan(serialize=serialize)

    def din(name, shape, dt):
        return nc.dram_tensor(name, list(shape), dt, kind="ExternalInput").ap()

    xT = din("xT", (C, N), F32)
    xoT = din("xoT", (C, OWN), F32)
    wqT = din("wqT", (PAIRS, 128, C), F32)
    wkT = din("wkT", (PAIRS, 128, C), F32)
    wvT = din("wvT", (PAIRS, 128, C), F32)
    gwT = din("gwT", (128, NCH * E), F32)
    lb_bc = din("lb_bc", (128, 16 * E), F32)
    egwT = din("egwT", (E, 128, NCH * I), F32)
    euwT = din("euwT", (E, 128, NCH * I), F32)
    edwT = din("edwT", (E, 128, 4 * C), F32)
    sgwT = din("sgwT", (128, NCH * SI), F32)
    suwT = din("suwT", (128, NCH * SI), F32)
    sdwT = din("sdwT", (128, NCH * C), F32)
    cosT = din("cosT", (128, N), BF16)
    sinT = din("sinT", (128, N), BF16)
    permT = din("permT", (128, 128), BF16)
    identc = din("identc", (128, 128), BF16)
    e2c = din("e2c", (128, 2), BF16)
    onesrow = din("onesrow", (1, 128), BF16)
    onescol = din("onescol", (128, 1), BF16)
    selAB = din("selAB", (2, 256), BF16)
    sel8 = din("sel8", (E, E * 128), BF16)
    v2init = din("v2init", (128, KCN * 130), BF16)

    outT = nc.dram_tensor("outT", [C, OWN], F32, kind="ExternalOutput").ap()
    ccin = nc.dram_tensor("ccin", [E, 2], F32)
    ccout = nc.dram_tensor("ccout", [E, 2], F32)

    dbg = {}

    def dump(tag, ap, waits):
        if not debug:
            return
        t = nc.dram_tensor(f"dbg_{tag}", list(ap.shape), ap.dtype,
                           kind="ExternalOutput").ap()
        dbg[tag] = True
        pl.dma("sync", t, ap, "o", waits=waits)

    ctx = ExitStack()
    with ctx:
        sb = lambda name, shape, dt: ctx.enter_context(
            nc.sbuf_tensor(name, list(shape), dt))
        psall = lambda name, shape, dt: ctx.enter_context(
            nc.psum_tensor(name, list(shape), dt))

        c_perm = sb("c_perm", (128, 128), BF16)
        c_id = sb("c_id", (128, 128), BF16)
        c_e2 = sb("c_e2", (128, 2), BF16)
        c_or = sb("c_or", (1, 128), BF16)
        c_oc = sb("c_oc", (128, 1), BF16)
        c_sel = sb("c_sel", (2, 256), BF16)
        c_sel8 = sb("c_sel8", (E, E * 128), BF16)
        c_lbbc = sb("c_lbbc", (128, 16 * E), F32)
        negb_bc = sb("negb_bc", (128, 2), F32)
        qmaxs = sb("qmaxs", (2, 32), F32)
        stats_sb = sb("stats_sb", (E, 2), F32)
        wmean_sb = sb("wmean_sb", (E, 4), F32)
        x_own = sb("x_own", (128, NCH * OWN), F32)

        SA = [psall("SA0", (128, 512), F32), psall("SA1", (128, 512), F32)]
        SBp = [psall("SB0", (128, 512), F32), psall("SB1", (128, 512), F32)]
        PA = psall("PA", (128, 512), F32)
        PB = psall("PB", (128, 512), F32)
        MISC = psall("MISC", (128, 512), F32)
        XTRA = psall("XTRA", (128, 1024), BF16)
        XTRAf = XTRA.ap().bitcast(F32)

        sems = {sname: ctx.enter_context(nc.semaphore("sem_" + sname))
                for sname in SEMS}

        ctxA = ctx.enter_context(ExitStack())
        sbA = lambda name, shape, dt: ctxA.enter_context(
            nc.sbuf_tensor(name, list(shape), dt))
        xn = sbA("xn", (128, NCH * N), BF16)
        cos2 = sbA("cos2", (128, N), BF16)
        sin2 = sbA("sin2", (128, N), BF16)
        k_raw = sbA("k_raw", (128, N), BF16)
        q_raw = sbA("q_raw", (128, OWN), BF16)
        v2 = sbA("v2", (128, KCN * 130), BF16)
        pTA = sbA("pTA", (128, 1024), BF16)
        pTB = sbA("pTB", (128, 1024), BF16)
        wq_sb = sbA("wq_sb", (128, NCH * 128), BF16)
        wk_sb = sbA("wk_sb", (128, NCH * 128), BF16)
        wv_sb = sbA("wv_sb", (128, NCH * 128), BF16)
        vt_sb = sbA("vt_sb", (128, 512), BF16)
        rope_tmp = sbA("rope_tmp", (128, 512), BF16)
        qsq = sbA("qsq", (128, 1024), BF16)
        recb_sb = sbA("recb_sb", (128, 1024), BF16)
        evactmp = sbA("evactmp", (128, 1024), F32)
        lnrows = sbA("lnrows", (1, 1536), F32)
        rb_rows = sbA("rb_rows", (1, 1024), BF16)
        rowtmp = sbA("rowtmp", (1, 512), F32)
        recrA = sbA("recrA", (1, 512), BF16)
        recrB = sbA("recrB", (1, 512), BF16)
        negb_b = sbA("negb_b", (2, 2), BF16)

        dsc = {}
        dsc["perm"] = pl.dma("sync", c_perm[:, :], permT[:, :], "ds")
        dsc["id"] = pl.dma("sync", c_id[:, :], identc[:, :], "ds")
        dsc["e2"] = pl.dma("sync", c_e2[:, :], e2c[:, :], "ds")
        dsc["or"] = pl.dma("sync", c_or[:, :], onesrow[:, :], "ds")
        dsc["oc"] = pl.dma("sync", c_oc[:, :], onescol[:, :], "ds")
        dsc["sel"] = pl.dma("sync", c_sel[:, :], selAB[:, :], "ds")
        dsc["sel8"] = pl.dma("sync", c_sel8[:, :], sel8[:, :], "ds")
        dsc["lb"] = pl.dma("sync", c_lbbc[:, :], lb_bc[:, :], "ds")
        dsc["cos"] = pl.dma("sync", cos2[:, :], cosT[:, :], "ds")
        dsc["sin"] = pl.dma("sync", sin2[:, :], sinT[:, :], "ds")
        dsc["v2i"] = pl.dma("sync", v2[:, :], v2init[:, :], "ds")
        chk_xo = [pl.dma("sync", x_own[:, OWN * c:OWN * (c + 1)],
                         xoT[128 * c:128 * (c + 1), :], "ds")
                  for c in range(NCH)]
        chk_xb = [pl.dma("gpsimd", xn[:, N * c:N * (c + 1)],
                         xT[128 * c:128 * (c + 1), :], "dg")
                  for c in range(NCH)]

        def wslice(dst, src, p, waits=None):
            return pl.dma("gpsimd", dst[:, :], src[p, :, :], "dg", waits=waits)

        chk_w = [None] * PAIRS
        chk_w[0] = (wslice(wq_sb, wqT, 0), wslice(wk_sb, wkT, 0),
                    wslice(wv_sb, wvT, 0))

        lnA, lnB, lnC = (lnrows[0:1, 0:512], lnrows[0:1, 512:1024],
                         lnrows[0:1, 1024:1536])
        misc_free_v = 0
        pa_free_v = 0
        sa_free_v = 0
        sa_free_s = 0
        sx2_t = [0, 0]
        for tg in range(N // 512):
            tg0 = 512 * tg
            for c in range(NCH):
                xin = xn[:, N * c + tg0: N * c + tg0 + 512]
                sl = qsq[:, 512 * (c % 2):512 * (c % 2) + 512]
                vchk = pl.tt(sl, xin, xin, OP.mult,
                             waits={"dg": chk_xb[c], "t": sx2_t[c % 2]})
                sx2_t[c % 2] = pl.mm(PA[0:1, 0:512], c_oc[:, :], sl,
                                     c == 0, c == NCH - 1,
                                     waits={"v": max(vchk, pa_free_v),
                                            "ds": dsc["oc"]}, inc=True)
            for c in range(NCH):
                xin = xn[:, N * c + tg0: N * c + tg0 + 512]
                pl.mm(MISC[0:1, 0:512], c_oc[:, :], xin, c == 0, c == NCH - 1,
                      waits={"dg": chk_xb[c], "v": misc_free_v},
                      inc=(c == NCH - 1))
            tchk = pl.cnt["t"]
            pl.ts(lnA, MISC[0:1, 0:512], 1.0 / C, OP.mult, waits={"t": tchk})
            pl.ts(lnB, PA[0:1, 0:512], 1.0 / C, OP.mult)
            pl.tt(lnC, lnA, lnA, OP.mult)
            pl.tt(lnB, lnB, lnC, OP.subtract)
            pl.ts(lnB, lnB, 1e-5, OP.add)
            pl.vrec(lnC, lnB)
            pl.vcopy(rb_rows[0:1, 0:512], lnC)
            v6 = pl.vcopy(rb_rows[0:1, 512:1024], lnA)
            misc_free_v = v6
            pa_free_v = v6
            t1 = pl.mm(SA[0].ap(), c_or[0:1, :], rb_rows[0:1, 0:512], True, True,
                       waits={"v": v6, "s": sa_free_s, "ds": dsc["or"]}, inc=True)
            t2 = pl.mm(SA[1].ap(), c_or[0:1, :], rb_rows[0:1, 512:1024],
                       True, True, inc=True)
            s7 = pl.act(recb_sb[:, 0:512], SA[0].ap(), ACT.Sqrt,
                        waits={"t": t1, "v": sa_free_v})
            sa_free_s = s7
            v7 = pl.tt(recb_sb[:, 512:1024], SA[1].ap(), recb_sb[:, 0:512],
                       OP.mult, waits={"t": t2, "s": s7})
            for c in range(NCH):
                xin = xn[:, N * c + tg0: N * c + tg0 + 512]
                pl.tt(xin, xin, recb_sb[:, 0:512], OP.mult, waits={"s": s7})
                sa_free_v = pl.tt(xin, xin, recb_sb[:, 512:1024], OP.subtract,
                                  waits={"v": v7})
        ln1_v = pl.cnt["v"]
        ln1_t = pl.cnt["t"]
        ln1_s = pl.cnt["s"]
        dump("xn", xn[:, :], {"v": ln1_v})
        if upto == "ln1":
            _close(pl, nc, ctx, ctxA, sems)
            return nc, dbg

        NPAIR = int(os.environ.get("MK_PAIRS", str(PAIRS)))
        expB_chk = {}
        prev_evac_v = ln1_v
        misc_free_s = ln1_s
        xtra_free_v = ln1_v
        xtra_free_s = ln1_s
        for p in range(NPAIR):
            wq_c, wk_c, wv_c = chk_w[p]
            miscb_free_v = prev_evac_v
            for tg in range(8):
                tg0 = 512 * tg
                for c in range(NCH):
                    pl.mm(MISC[:, :], wv_sb[:, 128 * c:128 * (c + 1)],
                          xn[:, N * c + tg0: N * c + tg0 + 512],
                          c == 0, c == NCH - 1,
                          waits={"dg": wv_c, "s": misc_free_s, "v": ln1_v},
                          inc=(c == NCH - 1))
                tchk = pl.cnt["t"]
                svt = pl.scopy(vt_sb[:, :], MISC[:, :], waits={"t": tchk})
                misc_free_s = svt
                for sub in range(4):
                    kc = 4 * tg + sub
                    half = sub % 2
                    if os.environ.get("MK_TROFF0"):
                        dst = XTRA[:, 0:128]
                    else:
                        dst = XTRA[:, 256 * half + 128 * (sub // 2):
                                   256 * half + 128 * (sub // 2) + 128]
                    wv_ = {"s": max(svt, xtra_free_s),
                           "v": max(xtra_free_v, miscb_free_v)}
                    ttr = pl.tr(dst, vt_sb[:, 128 * sub:128 * (sub + 1)],
                                c_id[:, :], waits=wv_)
                    pl.vcopy(v2[:, 130 * kc:130 * kc + 64], dst[:, 0:64],
                             waits={"t": ttr, "ds": dsc["v2i"]})
                    cB = pl.vcopy(v2[:, 130 * kc + 65:130 * kc + 129],
                                  dst[:, 64:128])
                    if half == 0:
                        xtra_free_v = cB
                    else:
                        miscb_free_v = cB
            xtra_free_v = max(xtra_free_v, miscb_free_v)
            vphase_v = pl.cnt["v"]
            if os.environ.get("MK_ATTN_SUB") == "v" and p == 0:
                dump("v2", v2[:, :], {"v": vphase_v})
                _close(pl, nc, ctx, ctxA, sems)
                return nc, dbg

            def proj_phase(dst, w_sb, w_chk, ntg, norm_off):
                nonlocal misc_free_s, xtra_free_v, xtra_free_s
                rope_v = 0
                nb_free_v = [prev_evac_v, prev_evac_v]
                sx_t = [ln1_t, ln1_t]
                for tg in range(ntg):
                    tg0 = 512 * tg
                    for c in range(NCH):
                        pl.mm(MISC[:, :], w_sb[:, 128 * c:128 * (c + 1)],
                              xn[:, N * c + tg0: N * c + tg0 + 512],
                              c == 0, c == NCH - 1,
                              waits={"dg": w_chk, "s": misc_free_s},
                              inc=(c == NCH - 1))
                    tchk = pl.cnt["t"]
                    tile = dst[:, tg0:tg0 + 512]
                    sq = pl.scopy(tile, MISC[:, :], waits={"t": tchk})
                    misc_free_s = sq
                    sl = qsq[:, 512 * (tg % 2):512 * (tg % 2) + 512]
                    vsq = pl.tt(sl, tile, tile, OP.mult,
                                waits={"s": sq, "t": sx_t[tg % 2]})
                    nb = SBp[tg % 2]
                    sx_t[tg % 2] = pl.mm(nb[0:2, 0:512], c_e2[:, :], sl, True, True,
                                         waits={"v": max(vsq, nb_free_v[tg % 2]),
                                                "ds": dsc["e2"]}, inc=True)
                    nb_free_v[tg % 2] = pl.vrmax(
                        qmaxs[0:2, norm_off + tg:norm_off + tg + 1],
                        nb[0:2, 0:512], waits={"t": sx_t[tg % 2]})
                    trm = pl.mm(XTRAf, c_perm[:, :], tile, True, True,
                                waits={"s": max(sq, xtra_free_s),
                                       "v": xtra_free_v, "ds": dsc["perm"]},
                                inc=True)
                    vv1 = pl.tt(rope_tmp[:, :], XTRAf,
                                sin2[:, tg0:tg0 + 512], OP.mult,
                                waits={"t": trm, "ds": dsc["sin"]})
                    pl.tt(tile, tile, cos2[:, tg0:tg0 + 512], OP.mult,
                          waits={"t": trm, "ds": dsc["cos"]})
                    rope_v = pl.tt(tile, tile, rope_tmp[:, :], OP.add)
                    xtra_free_v = vv1
                return rope_v

            rope_q_v = proj_phase(q_raw, wq_sb, wq_c, NQG, 0)
            rope_k_v = proj_phase(k_raw, wk_sb, wk_c, 8, 8)
            if os.environ.get("MK_ATTN_SUB") == "proj" and p == 0:
                dump("qraw", q_raw[:, :], {"v": rope_q_v})
                dump("kraw", k_raw[:, :], {"v": rope_k_v})
                _close(pl, nc, ctx, ctxA, sems)
                return nc, dbg

            vq = pl.vrmax(qmaxs[0:2, 30:31], qmaxs[0:2, 0:NQG])
            vk = pl.vrmax(qmaxs[0:2, 31:32], qmaxs[0:2, 8:16])
            vqk = pl.tt(qmaxs[0:2, 29:30], qmaxs[0:2, 30:31], qmaxs[0:2, 31:32],
                        OP.mult, waits={"v": vk})
            vb16 = pl.vcopy(negb_b[0:2, 0:1], qmaxs[0:2, 29:30])
            tb1 = pl.mm(MISC[:, 0:1], c_sel[:, 0:128], negb_b[0:2, 0:1],
                        True, True,
                        waits={"v": vb16, "s": misc_free_s, "ds": dsc["sel"]},
                        inc=True)
            tb2 = pl.mm(MISC[:, 1:2], c_sel[:, 128:256], negb_b[0:2, 0:1],
                        True, True, inc=True)
            ssq = pl.act(negb_bc[:, 0:2], MISC[:, 0:2], ACT.Sqrt,
                         scale=(0.125 * 1.013) ** 2, waits={"t": tb2})
            sbc_v = pl.v(lambda e, o=negb_bc[:, 0:2], i=negb_bc[:, 0:2]:
                         e.tensor_single_scalar(o, i, -1.0, OP.mult),
                         waits={"s": ssq})
            misc_free_s = ssq
            if os.environ.get("MK_ATTN_SUB") == "bounds" and p == 0:
                dump("negb", negb_bc[:, :], {"v": sbc_v})
                _close(pl, nc, ctx, ctxA, sems)
                return nc, dbg

            for qg in range(NQG):
                qg0 = QG * qg
                for kc in range(KCN):
                    w = {}
                    if qg == 0 and kc == 0:
                        w = {"v": max(rope_q_v, rope_k_v, vphase_v, sbc_v),
                             "s": misc_free_s}
                    key = (p, qg, kc - 2)
                    if kc < 2:
                        key = ((p, qg - 1, KCN - 2 + kc) if qg > 0
                               else (p - 1, NQG - 1, KCN - 2 + kc))
                    if key in expB_chk:
                        w["s"] = max(w.get("s", 0), expB_chk[key])
                    pl.mm(SA[kc % 2].ap(), k_raw[0:64, 128 * kc:128 * (kc + 1)],
                          q_raw[0:64, qg0:qg0 + QG], True, True, waits=w,
                          inc=True)
                    pl.mm(SBp[kc % 2].ap(), k_raw[64:128, 128 * kc:128 * (kc + 1)],
                          q_raw[64:128, qg0:qg0 + QG], True, True, inc=True)
                    if kc > 0:
                        wpv = {"s": expB_chk[(p, qg, kc - 1)]}
                        if kc == 1:
                            wpv["v"] = prev_evac_v
                        sl = 512 * ((kc - 1) % 2)
                        pl.mm(PA[0:65, 0:512],
                              v2[:, 130 * (kc - 1):130 * (kc - 1) + 65],
                              pTA[:, sl:sl + 512], kc - 1 == 0, False,
                              waits=wpv, inc=True)
                        pl.mm(PB[0:65, 0:512],
                              v2[:, 130 * (kc - 1) + 65:130 * (kc - 1) + 130],
                              pTB[:, sl:sl + 512], kc - 1 == 0, False,
                              inc=True)
                    gchk = pl.cnt["t"]
                    sl = 512 * (kc % 2)
                    pl.act(pTA[:, sl:sl + 512], SA[kc % 2].ap(), ACT.Exp,
                           bias=negb_bc[:, 0:1], scale=0.125,
                           waits={"t": gchk})
                    expB_chk[(p, qg, kc)] = pl.act(
                        pTB[:, sl:sl + 512], SBp[kc % 2].ap(), ACT.Exp,
                        bias=negb_bc[:, 1:2], scale=0.125)
                kc = KCN - 1
                sl = 512 * (kc % 2)
                pl.mm(PA[0:65, 0:512], v2[:, 130 * kc:130 * kc + 65],
                      pTA[:, sl:sl + 512], False, True,
                      waits={"s": expB_chk[(p, qg, kc)]}, inc=True)
                tpv = pl.mm(PB[0:65, 0:512], v2[:, 130 * kc + 65:130 * kc + 130],
                            pTB[:, sl:sl + 512], False, True, inc=True)
                vr1 = pl.vcopy(rowtmp[0:1, :], PA[64:65, 0:512], waits={"t": tpv})
                vr2 = pl.vrec(recrA[0:1, :], rowtmp[0:1, :])
                vr3 = pl.vcopy(rowtmp[0:1, :], PB[64:65, 0:512])
                vr4 = pl.vrec(recrB[0:1, :], rowtmp[0:1, :])
                tba = pl.mm(XTRAf, c_or[0:1, :], recrA[0:1, :], True, True,
                            waits={"v": max(vr2, xtra_free_v),
                                   "s": xtra_free_s}, inc=True)
                sca = pl.scopy(recb_sb[:, 0:512], XTRAf, waits={"t": tba})
                tbb = pl.mm(XTRAf, c_or[0:1, :], recrB[0:1, :], True, True,
                            waits={"v": vr4, "s": sca}, inc=True)
                scb = pl.scopy(recb_sb[:, 512:1024], XTRAf, waits={"t": tbb})
                xtra_free_s = scb
                xcol = OWN * p + qg0
                va = pl.tt(evactmp[0:64, 0:512], PA[0:64, 0:512],
                           recb_sb[0:64, 0:512], OP.mult, waits={"s": sca})
                vb = pl.tt(evactmp[0:64, 512:1024], PB[0:64, 0:512],
                           recb_sb[0:64, 512:1024], OP.mult, waits={"s": scb})
                vbs = pl.vcopy(evactmp[64:128, 512:1024], evactmp[0:64, 512:1024])
                pl.tt(x_own[0:64, xcol:xcol + QG], x_own[0:64, xcol:xcol + QG],
                      evactmp[0:64, 0:512], OP.add,
                      waits={"v": va, "ds": chk_xo[p]})
                prev_evac_v = pl.tt(x_own[64:128, xcol:xcol + QG],
                                    x_own[64:128, xcol:xcol + QG],
                                    evactmp[64:128, 512:1024], OP.add,
                                    waits={"v": vbs})
                xtra_free_v = prev_evac_v

            if p + 1 < NPAIR:
                cq = wslice(wq_sb, wqT, p + 1, waits={"s": misc_free_s})
                ck = wslice(wk_sb, wkT, p + 1)
                cv = wslice(wv_sb, wvT, p + 1)
                chk_w[p + 1] = (cq, ck, cv)

        attn_v = pl.cnt["v"]
        attn_t = pl.cnt["t"]
        attn_s = pl.cnt["s"]
        dump("xmid", x_own[:, :], {"v": attn_v})
        if upto == "attn":
            _close(pl, nc, ctx, ctxA, sems)
            return nc, dbg

        ctxA.__exit__(None, None, None)

        ctxB = ctx.enter_context(ExitStack())
        sbB = lambda name, shape, dt: ctxB.enter_context(
            nc.sbuf_tensor(name, list(shape), dt))
        xn2 = sbB("xn2", (128, NCH * OWN), BF16)
        wtT = sbB("wtT", (E, OWN), BF16)
        gw_sb = sbB("gw_sb", (128, NCH * E), BF16)
        s_all = sbB("s_all", (128, 128), F32)
        rt1 = sbB("rt1", (128, 8), F32)
        rt2 = sbB("rt2", (128, 8), F32)
        rmax1 = sbB("rmax1", (128, 1), F32)
        rmax2 = sbB("rmax2", (128, 1), F32)
        routbuf = sbB("routbuf", (128, 16), BF16)
        h_sb = sbB("h_sb", (128, 8192), BF16)
        sg_tmp = sbB("sg_tmp", (128, 512), BF16)
        wbc_sb = sbB("wbc_sb", (128, 2048), BF16)
        ffntmp = sbB("ffntmp", (128, 512), BF16)
        lnrB = sbB("lnrB", (1, 1536), F32)
        rbB = sbB("rbB", (1, 1024), BF16)
        bcB = sbB("bcB", (128, 1024), BF16)

        barrier = {"v": attn_v, "t": attn_t, "s": attn_s}
        chk_gw = pl.dma("gpsimd", gw_sb[:, :], gwT[:, :], "dg", waits=barrier)

        first = True
        for c in range(NCH):
            for halfc in range(2):
                src = x_own[:, OWN * c + 1024 * halfc: OWN * c + 1024 * (halfc + 1)]
                dst2 = xn2[:, OWN * c + 1024 * halfc: OWN * c + 1024 * (halfc + 1)]
                pl.vcopy(dst2, src, waits=barrier if first else None)
                first = False
        cpy_v = pl.cnt["v"]

        lnA2, lnB2, lnC2 = (lnrB[0:1, 0:512], lnrB[0:1, 512:1024],
                            lnrB[0:1, 1024:1536])
        misc_free_v = cpy_v
        pa_free_v = cpy_v
        sa_free_v = attn_v
        sa_free_s = attn_s
        sq_t = attn_t
        for tg in range(NQG):
            tg0 = 512 * tg
            for c in range(NCH):
                xin = xn2[:, OWN * c + tg0: OWN * c + tg0 + 512]
                vchk = pl.tt(ffntmp[:, :], xin, xin, OP.mult,
                             waits={"v": cpy_v, "t": sq_t})
                sq_t = pl.mm(PA[0:1, 0:512], c_oc[:, :], ffntmp[:, :],
                             c == 0, c == NCH - 1,
                             waits={"v": max(vchk, pa_free_v)}, inc=True)
            for c in range(NCH):
                xin = xn2[:, OWN * c + tg0: OWN * c + tg0 + 512]
                pl.mm(MISC[0:1, 0:512], c_oc[:, :], xin, c == 0, c == NCH - 1,
                      waits={"v": misc_free_v}, inc=(c == NCH - 1))
            tchk = pl.cnt["t"]
            pl.ts(lnA2, MISC[0:1, 0:512], 1.0 / C, OP.mult, waits={"t": tchk})
            pl.ts(lnB2, PA[0:1, 0:512], 1.0 / C, OP.mult)
            pl.tt(lnC2, lnA2, lnA2, OP.mult)
            pl.tt(lnB2, lnB2, lnC2, OP.subtract)
            pl.ts(lnB2, lnB2, 1e-5, OP.add)
            pl.vrec(lnC2, lnB2)
            pl.vcopy(rbB[0:1, 0:512], lnC2)
            v6 = pl.vcopy(rbB[0:1, 512:1024], lnA2)
            misc_free_v = v6
            pa_free_v = v6
            t1 = pl.mm(SA[0].ap(), c_or[0:1, :], rbB[0:1, 0:512], True, True,
                       waits={"v": v6, "s": sa_free_s}, inc=True)
            t2 = pl.mm(SA[1].ap(), c_or[0:1, :], rbB[0:1, 512:1024],
                       True, True, inc=True)
            s7 = pl.act(bcB[:, 0:512], SA[0].ap(), ACT.Sqrt,
                        waits={"t": t1, "v": sa_free_v})
            sa_free_s = s7
            v7 = pl.tt(bcB[:, 512:1024], SA[1].ap(), bcB[:, 0:512],
                       OP.mult, waits={"t": t2, "s": s7})
            for c in range(NCH):
                xin = xn2[:, OWN * c + tg0: OWN * c + tg0 + 512]
                pl.tt(xin, xin, bcB[:, 0:512], OP.mult, waits={"s": s7})
                sa_free_v = pl.tt(xin, xin, bcB[:, 512:1024], OP.subtract,
                                  waits={"v": v7})
        ln2_v = pl.cnt["v"]
        dump("xn2", xn2[:, :], {"v": ln2_v})

        part_t = 0
        for tc in range(OWN // 128):
            for c in range(NCH):
                pl.mm(MISC[0:128, E * tc:E * (tc + 1)],
                      xn2[:, OWN * c + 128 * tc: OWN * c + 128 * (tc + 1)],
                      gw_sb[:, E * c:E * (c + 1)], c == 0, c == NCH - 1,
                      waits={"dg": chk_gw, "v": misc_free_v, "s": misc_free_s},
                      inc=(c == NCH - 1 and tc == OWN // 128 - 1))
        tlg = pl.cnt["t"]
        vlg = pl.tt(s_all[:, :], MISC[0:128, 0:128], c_lbbc[:, :], OP.add,
                    waits={"t": tlg, "ds": dsc["lb"]})
        ssig = pl.act(s_all[:, :], s_all[:, :], ACT.Sigmoid, waits={"v": vlg})
        misc_free_v = vlg
        xtra_free_v2 = ln2_v
        for tc in range(OWN // 128):
            s_tm = s_all[:, E * tc:E * (tc + 1)]
            vm1 = pl.vrmax(rmax1[:, :], s_tm, waits={"s": ssig})
            veq = pl.ts(rt1[:, :], s_tm, rmax1[:, 0:1], OP.is_equal)
            pl.v(lambda e, o=rt2[:, :], a=rt1[:, :], b=s_tm:
                 e.scalar_tensor_tensor(o, a, -1e30, b, OP.mult, OP.add))
            vm2 = pl.vrmax(rmax2[:, :], rt2[:, :])
            vm = pl.ts(rt2[:, :], s_tm, rmax2[:, 0:1], OP.is_ge)
            pl.tt(rmax1[:, :], rmax1[:, :], rmax2[:, :], OP.add)
            pl.ts(rmax1[:, :], rmax1[:, :], 1e-8, OP.add)
            pl.vrec(rmax2[:, :], rmax1[:, :])
            pl.tt(rt1[:, :], rt2[:, :], s_tm, OP.mult)
            vmw = pl.ts(routbuf[:, 0:8], rt1[:, :], rmax2[:, 0:1], OP.mult,
                        waits={"t": part_t})
            vmm = pl.vcopy(routbuf[:, 8:16], rt2[:, :])
            pl.mm(PA[0:E, 0:1], routbuf[:, 0:8], c_oc[:, :],
                  tc == 0, tc == OWN // 128 - 1,
                  waits={"v": max(vmm, pa_free_v)})
            pl.mm(PB[0:E, 0:1], routbuf[:, 8:16], c_oc[:, :],
                  tc == 0, tc == OWN // 128 - 1,
                  waits={"v": prev_evac_v} if tc == 0 else None, inc=True)
            trm = pl.tr(XTRA[0:8, 0:128], routbuf[:, 8:16], c_id[:, :],
                        waits={"v": max(vmm, xtra_free_v2), "s": xtra_free_s})
            part_t = pl.cnt["t"]
            xtra_free_v2 = pl.vcopy(wtT[0:E, 128 * tc:128 * (tc + 1)],
                                    XTRA[0:E, 0:128], waits={"t": trm})
        pl.vcopy(stats_sb[:, 0:1], PA[0:E, 0:1], waits={"t": part_t})
        vst = pl.vcopy(stats_sb[:, 1:2], PB[0:E, 0:1])
        dcc1 = pl.dma("gpsimd", ccin.ap(), stats_sb[:, :], "dg",
                      waits={"v": vst})
        pl.emit("gpsimd",
                (lambda e: e.collective_compute(
                    "AllReduce", OP.add, replica_groups=[list(range(8))],
                    ins=[ccin.ap().opt()], outs=[ccout.ap().opt()])),
                waits={"dg": dcc1}, incs=[("c", 1)])
        dcc2 = pl.dma("gpsimd", stats_sb[:, :], ccout.ap(), "dg",
                      waits={"c": pl.cnt["c"]})
        vw1 = pl.ts(wmean_sb[:, 0:1], stats_sb[:, 1:2], 1.0, OP.max,
                    waits={"dg": dcc2})
        pl.vrec(wmean_sb[:, 1:2], wmean_sb[:, 0:1])
        vw3 = pl.tt(wmean_sb[:, 2:3], stats_sb[:, 0:1], wmean_sb[:, 1:2], OP.mult)
        vwt = pl.ts(wtT[:, :], wtT[:, :], wmean_sb[:, 2:3], OP.mult)
        dump("sall", s_all[:, :], {"v": pl.cnt["v"]})
        dump("rmax1", rmax1[:, :], {"v": pl.cnt["v"]})
        dump("rmax2", rmax2[:, :], {"v": pl.cnt["v"]})
        dump("rt1", rt1[:, :], {"v": pl.cnt["v"]})
        dump("rt2", rt2[:, :], {"v": pl.cnt["v"]})
        dump("routbuf", routbuf[:, :], {"v": pl.cnt["v"]})
        dump("stats", stats_sb[:, :], {"v": vw3})
        dump("wtT", wtT[:, :], {"v": vwt})
        if upto == "router":
            _close(pl, nc, ctx, ctxB, sems)
            return nc, dbg

        ctxC = ExitStack(); ctxC.__enter__()
        sg_sb = ctxC.enter_context(nc.sbuf_tensor("sg_sb", [128, NCH * SI], BF16))
        su_sb = ctxC.enter_context(nc.sbuf_tensor("su_sb", [128, NCH * SI], BF16))
        sd_sb = ctxC.enter_context(nc.sbuf_tensor("sd_sb", [128, NCH * C], BF16))
        csg = pl.dma("gpsimd", sg_sb[:, :], sgwT[:, :], "dg")
        csu = pl.dma("gpsimd", su_sb[:, :], suwT[:, :], "dg")
        csd = pl.dma("gpsimd", sd_sb[:, :], sdwT[:, :], "dg")

        silu_bank = [pl.cnt["s"], pl.cnt["s"]]
        up_bank_v = [ln2_v, ln2_v]
        add_bank_v = [prev_evac_v, prev_evac_v]
        ln2_guard = ln2_v
        for tg in range(NQG):
            tg0 = 512 * tg
            hdone = []
            for sic in range(NCH):
                g = sic % 2
                for c in range(NCH):
                    rhs = xn2[:, OWN * c + tg0: OWN * c + tg0 + 512]
                    pl.mm(SA[g].ap(),
                          sg_sb[:, SI * c + 128 * sic: SI * c + 128 * (sic + 1)],
                          rhs, c == 0, c == NCH - 1,
                          waits={"dg": csg, "s": silu_bank[g], "v": ln2_guard})
                    pl.mm(SBp[g].ap(),
                          su_sb[:, SI * c + 128 * sic: SI * c + 128 * (sic + 1)],
                          rhs, c == 0, c == NCH - 1,
                          waits={"dg": csu, "v": up_bank_v[g]},
                          inc=(c == NCH - 1))
                tchk = pl.cnt["t"]
                silu_bank[g] = pl.act(sg_tmp[:, :], SA[g].ap(), ACT.Silu,
                                      waits={"t": tchk})
                up_bank_v[g] = pl.tt(h_sb[:, 512 * sic:512 * (sic + 1)],
                                     SBp[g].ap(), sg_tmp[:, :], OP.mult,
                                     waits={"s": silu_bank[g]})
                hdone.append(up_bank_v[g])
            hv_tg = pl.cnt["v"]
            for dc in range(NCH):
                g = dc % 2
                db = PA if g == 0 else PB
                for sic in range(NCH):
                    pl.mm(db[0:128, 0:512],
                          sd_sb[:, C * sic + 128 * dc: C * sic + 128 * (dc + 1)],
                          h_sb[:, 512 * sic:512 * (sic + 1)],
                          sic == 0, sic == NCH - 1,
                          waits={"dg": csd, "v": max(hv_tg, add_bank_v[g])},
                          inc=(sic == NCH - 1))
                tchk = pl.cnt["t"]
                xsl = x_own[:, OWN * dc + tg0: OWN * dc + tg0 + 512]
                add_bank_v[g] = pl.tt(xsl, xsl, db[0:128, 0:512], OP.add,
                                      waits={"t": tchk})
            ln2_guard = pl.cnt["v"]
        shared_t = pl.cnt["t"]
        shared_s = pl.cnt["s"]
        ctxC.__exit__(None, None, None)

        ctxD = ExitStack(); ctxD.__enter__()
        eg_sb = ctxD.enter_context(nc.sbuf_tensor("eg_sb", [128, 2 * NCH * I], BF16))
        eu_sb = ctxD.enter_context(nc.sbuf_tensor("eu_sb", [128, 2 * NCH * I], BF16))
        ed_sb = ctxD.enter_context(nc.sbuf_tensor("ed_sb", [128, 2 * 4 * C], BF16))
        chk_e = [None] * E
        exp_done_t = [shared_t] * (E + 1)

        def prefetch(e_, wt):
            bb = e_ % 2
            c1 = pl.dma("gpsimd", eg_sb[:, NCH * I * bb: NCH * I * (bb + 1)],
                        egwT[e_, :, :], "dg", waits={"t": wt})
            c2 = pl.dma("gpsimd", eu_sb[:, NCH * I * bb: NCH * I * (bb + 1)],
                        euwT[e_, :, :], "dg")
            c3 = pl.dma("gpsimd", ed_sb[:, 4 * C * bb: 4 * C * (bb + 1)],
                        edwT[e_, :, :], "dg")
            chk_e[e_] = (c1, c2, c3)

        prefetch(0, shared_t)
        for e_ in range(E):
            bb = e_ % 2
            if e_ + 1 < E:
                prefetch(e_ + 1, exp_done_t[e_ - 1] if e_ > 0 else shared_t)
            c1, c2, c3 = chk_e[e_]
            wbc_s = []
            for tg in range(NQG):
                tb = pl.mm(MISC[:, :], c_sel8[:, 128 * e_:128 * (e_ + 1)],
                           wtT[0:E, 512 * tg:512 * (tg + 1)], True, True,
                           waits={"v": vwt, "s": misc_free_s,
                                  "ds": dsc["sel8"]}, inc=True)
                sc = pl.scopy(wbc_sb[:, 512 * tg:512 * (tg + 1)], MISC[:, :],
                              waits={"t": tb})
                misc_free_s = sc
                wbc_s.append(sc)
            hguard_t = exp_done_t[e_ - 1] if e_ > 0 else shared_t
            for ic in range(4):
                for tg in range(NQG):
                    tg0 = 512 * tg
                    g = (ic * NQG + tg) % 2
                    for c in range(NCH):
                        rhs = xn2[:, OWN * c + tg0: OWN * c + tg0 + 512]
                        pl.mm(SA[g].ap(),
                              eg_sb[:, NCH * I * bb + I * c + 128 * ic:
                                    NCH * I * bb + I * c + 128 * (ic + 1)],
                              rhs, c == 0, c == NCH - 1,
                              waits={"dg": c1, "s": silu_bank[g]})
                        pl.mm(SBp[g].ap(),
                              eu_sb[:, NCH * I * bb + I * c + 128 * ic:
                                    NCH * I * bb + I * c + 128 * (ic + 1)],
                              rhs, c == 0, c == NCH - 1,
                              waits={"dg": c2, "v": up_bank_v[g]},
                              inc=(c == NCH - 1))
                    tchk = pl.cnt["t"]
                    silu_bank[g] = pl.act(sg_tmp[:, :], SA[g].ap(), ACT.Silu,
                                          waits={"t": tchk})
                    v1 = pl.tt(ffntmp[:, :], SBp[g].ap(),
                               wbc_sb[:, 512 * tg:512 * (tg + 1)], OP.mult,
                               waits={"s": max(silu_bank[g], wbc_s[tg])})
                    up_bank_v[g] = v1
                    pl.tt(h_sb[:, 2048 * ic + tg0: 2048 * ic + tg0 + 512],
                          ffntmp[:, :], sg_tmp[:, :], OP.mult)
            h_done = pl.cnt["v"]
            for dc in range(NCH):
                for tg in range(NQG):
                    tg0 = 512 * tg
                    g = (dc * NQG + tg) % 2
                    db = PA if g == 0 else PB
                    for ic in range(4):
                        pl.mm(db[0:128, 0:512],
                              ed_sb[:, 4 * C * bb + C * ic + 128 * dc:
                                    4 * C * bb + C * ic + 128 * (dc + 1)],
                              h_sb[:, 2048 * ic + tg0: 2048 * ic + tg0 + 512],
                              ic == 0, ic == 3,
                              waits={"dg": c3, "v": max(h_done, add_bank_v[g])},
                              inc=(ic == 3))
                    tchk = pl.cnt["t"]
                    xsl = x_own[:, OWN * dc + tg0: OWN * dc + tg0 + 512]
                    add_bank_v[g] = pl.tt(xsl, xsl, db[0:128, 0:512], OP.add,
                                          waits={"t": tchk})
            exp_done_t[e_] = pl.cnt["t"]
        ctxD.__exit__(None, None, None)

        final_v = pl.cnt["v"]
        for c in range(NCH):
            pl.dma("sync", outT[128 * c:128 * (c + 1), :],
                   x_own[:, OWN * c:OWN * (c + 1)], "o",
                   waits={"v": final_v})
        _close(pl, nc, ctx, ctxB, sems)
    return nc, dbg


def _close(pl, nc, ctx, inner_ctx, sems):
    total_o = pl.cnt["o"]
    if total_o:
        pl.emit("sync", (lambda e: e.nop()), waits={"o": total_o})
    block = ctx.enter_context(nc.Block())
    for ename in ENGINES:
        steps = pl.steps[ename]
        if not steps:
            continue

        def run(eng, steps=steps):
            last = {}
            for fn, waits, incs in steps:
                for sname, val in waits.items():
                    if val > last.get(sname, 0):
                        eng.wait_ge(sems[sname], val)
                        last[sname] = val
                ins = fn(eng)
                for sname, n in incs:
                    ins = ins.then_inc(sems[sname], n)
        getattr(block, ename)(run)



_BUILD_CACHE = {}


def _get_program(upto="full", serialize=None, debug=False):
    if serialize is None:
        serialize = bool(int(os.environ.get("MK_SERIALIZE", "1")))
    key = (upto, serialize, debug)
    if key not in _BUILD_CACHE:
        _BUILD_CACHE[key] = build_program(upto=upto, serialize=serialize,
                                          debug=debug)
    return _BUILD_CACHE[key]


def _rope_tables(n, d):
    pos = np.arange(n, dtype=np.float64)
    inv = np.power(10000.0, -2.0 * np.arange(d // 2) / d)
    ang = pos[:, None] * inv[None, :]
    cos = np.repeat(np.cos(ang), 2, axis=-1)
    sin = np.repeat(np.sin(ang), 2, axis=-1)
    return cos.astype(np.float32), sin.astype(np.float32)


def _consts():
    cos, sin = _rope_tables(N, D)
    cosT = np.ascontiguousarray(cos.T)
    sinT = np.ascontiguousarray(sin.T)
    cos2 = np.concatenate([cosT, cosT], axis=0).astype(ml_dtypes.bfloat16)
    sin2 = np.concatenate([sinT, sinT], axis=0).astype(ml_dtypes.bfloat16)
    p = np.zeros((128, 128), np.float32)
    for blk in range(2):
        o = 64 * blk
        for i in range(32):
            p[o + 2 * i, o + 2 * i + 1] = -1.0
            p[o + 2 * i + 1, o + 2 * i] = 1.0
    permT = np.ascontiguousarray(p.T).astype(ml_dtypes.bfloat16)
    ident = np.eye(128, dtype=ml_dtypes.bfloat16)
    e2 = np.zeros((128, 2), np.float32)
    e2[0:64, 0] = 1.0
    e2[64:128, 1] = 1.0
    selAB = np.zeros((2, 256), np.float32)
    selAB[0, 0:128] = 1.0
    selAB[1, 128:256] = 1.0
    sel8 = np.zeros((E, E * 128), np.float32)
    for e_ in range(E):
        sel8[e_, 128 * e_:128 * (e_ + 1)] = 1.0
    v2i = np.zeros((128, KCN * 130), np.float32)
    v2i[:, 64::130] = 1.0
    v2i[:, 129::130] = 1.0
    return dict(cosT=cos2, sinT=sin2, permT=permT, identc=ident,
                e2c=e2.astype(ml_dtypes.bfloat16),
                onesrow=np.ones((1, 128), dtype=ml_dtypes.bfloat16),
                onescol=np.ones((128, 1), dtype=ml_dtypes.bfloat16),
                selAB=selAB.astype(ml_dtypes.bfloat16),
                sel8=sel8.astype(ml_dtypes.bfloat16),
                v2init=v2i.astype(ml_dtypes.bfloat16))


def make_in_maps(inputs):
    f = lambda a: np.ascontiguousarray(np.asarray(a, dtype=np.float32))
    x = f(inputs["L0_attn_x"])
    lb = f(inputs["L0_ffn_lb_bias"]).reshape(1, E)

    def pairify(wT):
        w4 = wT.reshape(8, 128, 8, 128)
        return np.ascontiguousarray(w4.transpose(2, 1, 0, 3).reshape(8, 128, 1024))

    def chunkify(wT, nch):
        M = wT.shape[1]
        return np.ascontiguousarray(
            wT.reshape(nch, 128, M).transpose(1, 0, 2).reshape(128, nch * M))

    shared = dict(
        wqT=pairify(f(np.asarray(inputs["L0_attn_Wq"]).T)),
        wkT=pairify(f(np.asarray(inputs["L0_attn_Wk"]).T)),
        wvT=pairify(f(np.asarray(inputs["L0_attn_Wv"]).T)),
        gwT=chunkify(f(np.asarray(inputs["L0_ffn_gate_w"]).T), 8),
        lb_bc=np.ascontiguousarray(np.tile(lb, (128, 16))),
        egwT=np.stack([chunkify(f(np.asarray(inputs["L0_ffn_expert_gate_w"][e]).T), 8)
                       for e in range(E)]),
        euwT=np.stack([chunkify(f(np.asarray(inputs["L0_ffn_expert_up_w"][e]).T), 8)
                       for e in range(E)]),
        edwT=np.stack([chunkify(f(np.asarray(inputs["L0_ffn_expert_down_w"][e]).T), 4)
                       for e in range(E)]),
        sgwT=chunkify(f(np.asarray(inputs["L0_ffn_shared_gate_w"]).T), 8),
        suwT=chunkify(f(np.asarray(inputs["L0_ffn_shared_up_w"]).T), 8),
        sdwT=chunkify(f(np.asarray(inputs["L0_ffn_shared_down_w"]).T), 8),
    )
    cst = _consts()
    base = dict(cst, **shared)
    cos2, sin2 = cst["cosT"], cst["sinT"]
    cos2b = np.ascontiguousarray(
        np.concatenate([cos2[:, OWN:], cos2[:, :OWN]], axis=1))
    sin2b = np.ascontiguousarray(
        np.concatenate([sin2[:, OWN:], sin2[:, :OWN]], axis=1))
    in_maps = []
    for c in range(8):
        b, h = c // 2, c % 2
        xb = x[b]
        if h == 0:
            xp = xb
            ct, st = cos2, sin2
        else:
            xp = np.concatenate([xb[OWN:], xb[:OWN]], axis=0)
            ct, st = cos2b, sin2b
        m = dict(base)
        m["xT"] = np.ascontiguousarray(xp.T)
        m["xoT"] = np.ascontiguousarray(xp[:OWN].T)
        m["cosT"] = ct
        m["sinT"] = st
        in_maps.append(m)
    return in_maps


def kernel(**inputs):
    nc, _ = _get_program()
    in_maps = make_in_maps(inputs)
    res = run_bass_kernel_spmd(nc, in_maps, core_ids=list(range(8)))
    out = np.empty((B, N, C), dtype=np.float32)
    for c in range(8):
        b, h = c // 2, c % 2
        out[b, OWN * h:OWN * (h + 1), :] = res.results[c]["outT"].T
    return out
